# version 03
# baseline (speedup 1.0000x reference)
import numpy as np
import ml_dtypes
from contextlib import ExitStack

import concourse.bass as bass
import concourse.mybir as mybir
from concourse.bass_utils import run_bass_kernel_spmd

F32 = mybir.dt.float32
BF16 = mybir.dt.bfloat16
AF = mybir.ActivationFunctionType
ALU = mybir.AluOpType
AX = mybir.AxisListType
BF = ml_dtypes.bfloat16

NC = 8
D = 2048
TOK = 8192
TPC = 1024
S = 2048
EPS = 1e-5


class Prog:
    ENGS = ("sync", "tensor", "vector", "scalar", "gpsimd")

    def __init__(self):
        self.nc = bass.Bass("TRN2", target_bir_lowering=False)
        self.outer = ExitStack()
        self.sem = self.outer.enter_context(self.nc.semaphore("G"))
        self.ncc = 0
        self.items = []
        self.v = 0

    def _add(self, eng, fn, inc):
        self.items.append((eng, self.v, fn, inc))
        self.v += inc

    def dma(self, pairs):
        pairs = list(pairs)

        def fn(e, pairs=pairs):
            return [e.dma_start(out=o, in_=i) for (o, i) in pairs]

        self._add("sync", fn, 16 * len(pairs))

    def mm(self, lst):
        lst = list(lst)

        def fn(e, lst=lst):
            r = None
            for (o, l, rr, st, sp) in lst:
                r = e.matmul(o, l, rr, start=st, stop=sp)
            return r

        self._add("tensor", fn, 1)

    def tr(self, lst):
        lst = list(lst)

        def fn(e, lst=lst):
            r = None
            for (o, i, idn) in lst:
                r = e.transpose(o, i, idn)
            return r

        self._add("tensor", fn, 1)

    def op(self, eng, meth, *a, **k):
        def fn(e, m=meth, a=a, k=k):
            return getattr(e, m)(*a, **k)

        self._add(eng, fn, 1)

    def collective(self, kind, op, rg, ins, outs):
        cc = self.outer.enter_context(self.nc.semaphore(f"cc{self.ncc}"))
        self.ncc += 1
        gsem = self.sem

        def fn(e, cc=cc, kind=kind, op=op, rg=rg, ins=ins, outs=outs, gsem=gsem):
            e.collective_compute(kind, op, replica_groups=rg,
                                 ins=ins, outs=outs).then_inc(cc, 16)
            e.wait_ge(cc, 1)
            e.sem_inc(gsem, 1)
            return None

        self._add("gpsimd", fn, 1)

    def flush(self, stage_stack=None):
        items = self.items
        self.items = []
        sem = self.sem
        with self.nc.Block() as blk:
            used = {e for (e, _, _, _) in items}
            for name in self.ENGS:
                if name not in used:
                    continue

                def runner(eng, name=name, items=items, sem=sem):
                    for (en, wait_at, fn, inc) in items:
                        if en != name:
                            continue
                        eng.wait_ge(sem, wait_at)
                        r = fn(eng)
                        if r is None:
                            continue
                        if isinstance(r, list):
                            for inst in r:
                                inst.then_inc(sem, 16)
                        else:
                            r.then_inc(sem, inc)

                getattr(blk, name)(runner)
        if stage_stack is not None:
            stage_stack.close()

    def done(self):
        sem = self.sem
        vfin = self.v
        with self.nc.Block() as blk:
            def runner(eng, sem=sem, vfin=vfin):
                eng.wait_ge(sem, vfin)

            blk.sync(runner)
        self.outer.close()
        return self.nc


def layernorm(p, out, xin, sq, msum, m, vs, sd, rstd, dim, eps_ap):
    p.op("vector", "reduce_sum", msum, xin, AX.X)
    p.op("vector", "tensor_scalar", m, msum, 1.0 / dim, None, ALU.mult)
    p.op("vector", "tensor_scalar", out, xin, m, None, ALU.subtract)
    p.op("scalar", "activation", sq, out, AF.Square)
    p.op("vector", "reduce_sum", vs, sq, AX.X)
    p.op("vector", "tensor_scalar", vs, vs, 1.0 / dim, None, ALU.mult)
    p.op("vector", "tensor_tensor", vs, vs, eps_ap, ALU.add)
    p.op("scalar", "activation", sd, vs, AF.Sqrt)
    p.op("vector", "reciprocal", rstd, sd)
    p.op("vector", "tensor_scalar", out, out, rstd, None, ALU.mult)


def elup1(p, out, xin, scratch):
    p.op("vector", "tensor_scalar", out, xin, 0.0, None, ALU.min)
    p.op("scalar", "activation", out, out, AF.Exp)
    p.op("vector", "tensor_scalar", scratch, xin, 0.0, None, ALU.max)
    p.op("vector", "tensor_tensor", out, out, scratch, ALU.add)



def build_ABC():
    p = Prog()
    nc = p.nc
    _uid = [0]

    def _mk(stack, kind, n, s, d):
        _uid[0] += 1
        f = nc.sbuf_tensor if kind == "sb" else nc.psum_tensor
        return stack.enter_context(f(f"{n}_{_uid[0]}", s, d))

    EI = lambda n, s, d: nc.dram_tensor(n, s, d, kind="ExternalInput").ap()
    IN = lambda n, s, d: nc.dram_tensor(n, s, d, kind="Internal").ap()
    x_in = EI("x", [S, D], F32)
    cosn = EI("cosn", [S, D], F32)
    sinn = EI("sinn", [S, D], F32)
    cost = EI("cost", [D, TPC], F32)
    sint = EI("sint", [D, TPC], F32)
    costb = EI("costb", [D, S], F32)
    sintb = EI("sintb", [D, S], F32)
    wqt = EI("wqt", [D, D], F32)
    wkt = EI("wkt", [D, D], F32)
    wvt = EI("wvt", [D, D], F32)
    wot = EI("wot", [D, D], F32)
    gate = EI("gate", [16 * 128, 1], F32)
    idb = EI("identb", [128, 128], BF16)
    x1o = nc.dram_tensor("x1o", [TPC, D], F32, kind="ExternalOutput").ap()
    ln1o = nc.dram_tensor("ln1o", [D, TPC], BF16, kind="ExternalOutput").ap()

    qTl = IN("qTl", [D, TPC], BF16)
    kTl = IN("kTl", [D, S], BF16)
    kNl = IN("kNl", [S, D], BF16)
    vNl = IN("vNl", [S, D], BF16)

    per = ExitStack()
    idb_sb = per.enter_context(nc.sbuf_tensor("idb_sb", [128, 128], BF16))
    eps_sb = per.enter_context(nc.sbuf_tensor("eps_sb", [128, 1], F32))
    p.dma([(idb_sb[:, :], idb)])
    p.op("vector", "memset", eps_sb[:, :], EPS)

    st = ExitStack()
    sb = lambda n, s, d: _mk(st, "sb", n, s, d)
    ps = lambda n, s, d: _mk(st, "ps", n, s, d)
    xt = sb("xt", [128, D], F32)
    ln = sb("ln", [128, D], F32)
    msum = sb("msum", [128, 1], F32)
    m = sb("m", [128, 1], F32)
    vs = sb("vs", [128, 1], F32)
    sd = sb("sd", [128, 1], F32)
    rstd = sb("rstd", [128, 1], F32)
    ln_bf = sb("ln_bf", [128, D], BF16)
    lnT = sb("lnT", [128, 16, S], BF16)
    wtmp = sb("wtmp", [128, D], F32)
    wbf = sb("wbf", [128, 16, D], BF16)
    s0 = sb("s0", [128, TPC], F32)
    s1 = sb("s1", [128, TPC], F32)
    r0 = sb("r0", [128, TPC], F32)
    r1 = sb("r1", [128, TPC], F32)
    tmp = sb("tmp", [128, D], F32)
    c0 = sb("c0", [128, TPC], F32)
    c1 = sb("c1", [128, TPC], F32)
    si0 = sb("si0", [128, TPC], F32)
    si1 = sb("si1", [128, TPC], F32)
    q_ps = ps("q_ps", [128, D], F32)
    tp_ps = ps("tp_ps", [128, 16, 128], BF16)

    for tt in range(16):
        p.dma([(xt[:, :], x_in[tt * 128:(tt + 1) * 128, :])])
        layernorm(p, ln[:, :], xt[:, :], wtmp[:, :], msum[:, :], m[:, :],
                  vs[:, :], sd[:, :], rstd[:, :], D, eps_sb[:, :])
        p.op("vector", "tensor_copy", ln_bf[:, :], ln[:, :])
        p.tr([(tp_ps[:, j, :], ln_bf[:, j * 128:(j + 1) * 128], idb_sb[:, :])
              for j in range(16)])
        p.op("vector", "tensor_copy", lnT[:, :, tt * 128:(tt + 1) * 128], tp_ps[:, :, :])

    wr = wqt.rearrange("(a p) d -> p a d", p=128)
    for i in range(16):
        p.dma([(wtmp[:, :], wr[:, i, :])])
        p.op("vector", "tensor_copy", wbf[:, i, :], wtmp[:, :])
    for i in range(8):
        for (dst, dch) in ((s0, i), (s1, i + 8)):
            for tc in range(2):
                p.mm([(q_ps[:, tc * 512:(tc + 1) * 512],
                       wbf[:, ht, dch * 128:(dch + 1) * 128],
                       lnT[:, ht, tc * 512:(tc + 1) * 512],
                       ht == 0, ht == 15) for ht in range(16)])
            p.op("vector", "tensor_copy", dst[:, :], q_ps[:, 0:TPC])
        p.dma([(c0[:, :], cost[i * 128:(i + 1) * 128, :]),
               (c1[:, :], cost[(i + 8) * 128:(i + 9) * 128, :]),
               (si0[:, :], sint[i * 128:(i + 1) * 128, :]),
               (si1[:, :], sint[(i + 8) * 128:(i + 9) * 128, :])])
        p.op("vector", "tensor_tensor", r0[:, :], s0[:, :], c0[:, :], ALU.mult)
        p.op("vector", "tensor_tensor", tmp[:, 0:TPC], s1[:, :], si0[:, :], ALU.mult)
        p.op("vector", "tensor_tensor", r0[:, :], r0[:, :], tmp[:, 0:TPC], ALU.subtract)
        p.op("vector", "tensor_tensor", r1[:, :], s1[:, :], c1[:, :], ALU.mult)
        p.op("vector", "tensor_tensor", tmp[:, 0:TPC], s0[:, :], si1[:, :], ALU.mult)
        p.op("vector", "tensor_tensor", r1[:, :], r1[:, :], tmp[:, 0:TPC], ALU.add)
        p.op("vector", "tensor_copy", ln_bf[:, 0:TPC], r0[:, :])
        p.dma([(qTl[i * 128:(i + 1) * 128, :], ln_bf[:, 0:TPC])])
        p.op("vector", "tensor_copy", ln_bf[:, 0:TPC], r1[:, :])
        p.dma([(qTl[(i + 8) * 128:(i + 9) * 128, :], ln_bf[:, 0:TPC])])

    wr = wkt.rearrange("(a p) d -> p a d", p=128)
    for i in range(16):
        p.dma([(wtmp[:, :], wr[:, i, :])])
        p.op("vector", "tensor_copy", wbf[:, i, :], wtmp[:, :])
    for half in range(2):
        hof = half * TPC
        for i in range(8):
            for (dst, dch) in ((s0, i), (s1, i + 8)):
                for tc in range(2):
                    p.mm([(q_ps[:, tc * 512:(tc + 1) * 512],
                           wbf[:, ht, dch * 128:(dch + 1) * 128],
                           lnT[:, ht, hof + tc * 512:hof + (tc + 1) * 512],
                           ht == 0, ht == 15) for ht in range(16)])
                p.op("vector", "tensor_copy", dst[:, :], q_ps[:, 0:TPC])
            p.dma([(c0[:, :], costb[i * 128:(i + 1) * 128, hof:hof + TPC]),
                   (c1[:, :], costb[(i + 8) * 128:(i + 9) * 128, hof:hof + TPC]),
                   (si0[:, :], sintb[i * 128:(i + 1) * 128, hof:hof + TPC]),
                   (si1[:, :], sintb[(i + 8) * 128:(i + 9) * 128, hof:hof + TPC])])
            p.op("vector", "tensor_tensor", r0[:, :], s0[:, :], c0[:, :], ALU.mult)
            p.op("vector", "tensor_tensor", tmp[:, 0:TPC], s1[:, :], si0[:, :], ALU.mult)
            p.op("vector", "tensor_tensor", r0[:, :], r0[:, :], tmp[:, 0:TPC], ALU.subtract)
            p.op("vector", "tensor_tensor", r1[:, :], s1[:, :], c1[:, :], ALU.mult)
            p.op("vector", "tensor_tensor", tmp[:, 0:TPC], s0[:, :], si1[:, :], ALU.mult)
            p.op("vector", "tensor_tensor", r1[:, :], r1[:, :], tmp[:, 0:TPC], ALU.add)
            p.op("vector", "tensor_copy", ln_bf[:, 0:TPC], r0[:, :])
            p.dma([(kTl[i * 128:(i + 1) * 128, hof:hof + TPC], ln_bf[:, 0:TPC])])
            p.op("vector", "tensor_copy", ln_bf[:, 0:TPC], r1[:, :])
            p.dma([(kTl[(i + 8) * 128:(i + 9) * 128, hof:hof + TPC], ln_bf[:, 0:TPC])])

    for (wdram, outdram, do_rope) in ((wkt, kNl, True), (wvt, vNl, False)):
        wr = wdram.rearrange("(a p) d -> p a d", p=128)
        for i in range(16):
            p.dma([(wtmp[:, :], wr[:, i, :])])
            p.op("vector", "tensor_copy", wbf[:, i, :], wtmp[:, :])
        for tt in range(16):
            for ch in range(4):
                p.mm([(q_ps[:, ch * 512:(ch + 1) * 512],
                       lnT[:, ht, tt * 128:(tt + 1) * 128],
                       wbf[:, ht, ch * 512:(ch + 1) * 512],
                       ht == 0, ht == 15) for ht in range(16)])
            p.op("vector", "tensor_copy", tmp[:, :], q_ps[:, :])
            if do_rope:
                p.dma([(c0[:, :], cosn[tt * 128:(tt + 1) * 128, 0:TPC]),
                       (c1[:, :], cosn[tt * 128:(tt + 1) * 128, TPC:D]),
                       (si0[:, :], sinn[tt * 128:(tt + 1) * 128, 0:TPC]),
                       (si1[:, :], sinn[tt * 128:(tt + 1) * 128, TPC:D])])
                p.op("vector", "tensor_tensor", r0[:, :], tmp[:, 0:TPC], c0[:, :], ALU.mult)
                p.op("vector", "tensor_tensor", s0[:, :], tmp[:, TPC:D], si0[:, :], ALU.mult)
                p.op("vector", "tensor_tensor", r0[:, :], r0[:, :], s0[:, :], ALU.subtract)
                p.op("vector", "tensor_tensor", r1[:, :], tmp[:, TPC:D], c1[:, :], ALU.mult)
                p.op("vector", "tensor_tensor", s1[:, :], tmp[:, 0:TPC], si1[:, :], ALU.mult)
                p.op("vector", "tensor_tensor", r1[:, :], r1[:, :], s1[:, :], ALU.add)
                p.op("vector", "tensor_copy", ln_bf[:, 0:TPC], r0[:, :])
                p.op("vector", "tensor_copy", ln_bf[:, TPC:D], r1[:, :])
            else:
                p.op("vector", "tensor_copy", ln_bf[:, :], tmp[:, :])
            p.dma([(outdram[tt * 128:(tt + 1) * 128, :], ln_bf[:, :])])
    p.flush(st)
    st = ExitStack()
    sb = lambda n, s, d: _mk(st, "sb", n, s, d)
    ps = lambda n, s, d: _mk(st, "ps", n, s, d)
    SC0 = 1.0 / float(np.sqrt(128.0))
    qT_bf = sb("qT_bf", [128, TPC], BF16)
    kT_bf = sb("kT_bf", [128, S], BF16)
    kn_bf = sb("kn_bf", [128, 16, 128], BF16)
    v_bf = sb("v_bf", [128, 16, 128], BF16)
    kf = sb("kf", [128, 16, 128], F32)
    kf_bf = sb("kf_bf", [128, 16, 128], BF16)
    qf = sb("qf", [128, TPC], F32)
    scr = sb("scr", [128, S], F32)
    scr2 = sb("scr2", [128, 128], F32)
    mem_rhs = sb("mem_rhs", [128, 129], F32)
    ones = sb("ones", [128, 1], BF16)
    idf_sb = sb("idf_sb", [128, 128], F32)
    g_in = sb("g_in", [128, 1], F32)
    g_sb = sb("g_sb", [128, 1], F32)
    P_sb = sb("P_sb", [128, S], BF16)
    PT_sb = sb("PT_sb", [128, 16, 128], BF16)
    rm = sb("rm", [128, 1], F32)
    rmb = sb("rmb", [128, 1], F32)
    den = sb("den", [128, 1], F32)
    dn = sb("dn", [128, 1], F32)
    dr = sb("dr", [128, 1], F32)
    mo_sb = sb("mo_sb", [128, 128], F32)
    attn_sb = sb("attn_sb", [128, 128], F32)
    dsb = sb("dsb", [128, 128], F32)
    comb_all = sb("comb_all", [128, 8, D], BF16)
    wtmp = sb("wtmp", [128, D], F32)
    wbf = sb("wbf", [128, 16, D], BF16)
    ctile = sb("ctile", [128, 16, 128], BF16)
    xt = sb("xt", [128, D], F32)
    x1_sb = sb("x1_sb", [128, D], F32)
    ln = sb("ln", [128, D], F32)
    msum = sb("msum", [128, 1], F32)
    m = sb("m", [128, 1], F32)
    vs = sb("vs", [128, 1], F32)
    sd = sb("sd", [128, 1], F32)
    rstd = sb("rstd", [128, 1], F32)
    ln_bf = sb("ln_bf", [128, D], BF16)
    lt_sb = sb("lt_sb", [128, 16, 128], BF16)
    kn_f3 = xt[:, :].rearrange("p (a b) -> p a b", a=16)
    scr3v = ln[:, :].rearrange("p (a b) -> p a b", a=16)
    sc_ps = ps("sc_ps", [128, S], F32)
    tp_ps = ps("tp_ps", [128, 16, 128], BF16)
    misc = ps("misc", [128, 512], F32)

    p.op("vector", "memset", ones[:, :], 1.0)
    p.op("vector", "tensor_copy", idf_sb[:, :], idb_sb[:, :])

    for h in range(16):
        hs = slice(h * 128, (h + 1) * 128)
        p.dma([
            (qT_bf[:, :], qTl[hs, :]),
            (kT_bf[:, :], kTl[hs, :]),
            (kn_bf[:, :, :], kNl[:, hs].rearrange("(t q) e -> q t e", q=128)),
            (v_bf[:, :, :], vNl[:, hs].rearrange("(t q) e -> q t e", q=128)),
            (g_in[:, :], gate[h * 128:(h + 1) * 128, :]),
        ])
        p.op("scalar", "activation", g_sb[:, :], g_in[:, :], AF.Sigmoid)
        p.op("vector", "tensor_copy", kn_f3, kn_bf[:, :, :])
        elup1(p, kf[:, :, :], kn_f3, scr3v)
        p.op("vector", "tensor_copy", kf_bf[:, :, :], kf[:, :, :])
        p.op("vector", "tensor_copy", scr[:, 0:TPC], qT_bf[:, :])
        elup1(p, qf[:, :], scr[:, 0:TPC], scr[:, TPC:S])
        p.mm([(misc[:, 0:128], v_bf[:, i, :], kf_bf[:, i, :], i == 0, i == 15)
              for i in range(16)])
        p.mm([(misc[:, 128:129], kf_bf[:, i, :], ones[:, :], i == 0, i == 15)
              for i in range(16)])
        p.op("vector", "tensor_copy", scr2[:, :], misc[:, 0:128])
        p.tr([(misc[:, 256:384], scr2[:, :], idf_sb[:, :])])
        p.op("vector", "tensor_copy", mem_rhs[:, 0:128], misc[:, 256:384])
        p.op("vector", "tensor_copy", mem_rhs[:, 128:129], misc[:, 128:129])
        for c in range(8):
            cs = slice(c * 128, (c + 1) * 128)
            p.mm([(misc[:, 0:129], qf[:, cs], mem_rhs[:, 0:129], True, True)])
            p.op("vector", "reciprocal", dn[:, :], misc[:, 128:129])
            p.op("vector", "tensor_scalar", mo_sb[:, :], misc[:, 0:128], dn[:, :], None, ALU.mult)
            p.mm([(sc_ps[:, j * 512:(j + 1) * 512], qT_bf[:, cs],
                   kT_bf[:, j * 512:(j + 1) * 512], True, True) for j in range(4)])
            p.op("vector", "reduce_max", rm[:, :], sc_ps[:, :], AX.X, negate=True)
            p.op("vector", "tensor_scalar", rmb[:, :], rm[:, :], SC0, None, ALU.mult)
            p.op("scalar", "activation", P_sb[:, :], sc_ps[:, :], AF.Exp,
                 bias=rmb[:, :], scale=SC0)
            p.op("vector", "reduce_sum", den[:, :], P_sb[:, :], AX.X)
            p.tr([(tp_ps[:, j, :], P_sb[:, j * 128:(j + 1) * 128], idb_sb[:, :])
                  for j in range(16)])
            p.op("vector", "tensor_copy", PT_sb[:, :, :], tp_ps[:, :, :])
            p.mm([(misc[:, 256:384], PT_sb[:, j, :], v_bf[:, j, :], j == 0, j == 15)
                  for j in range(16)])
            p.op("vector", "reciprocal", dr[:, :], den[:, :])
            p.op("vector", "tensor_scalar", attn_sb[:, :], misc[:, 256:384], dr[:, :], None, ALU.mult)
            p.op("vector", "tensor_tensor", dsb[:, :], mo_sb[:, :], attn_sb[:, :], ALU.subtract)
            p.op("vector", "tensor_scalar", dsb[:, :], dsb[:, :], g_sb[:, :], None, ALU.mult)
            p.op("vector", "tensor_tensor", comb_all[:, c, hs], attn_sb[:, :], dsb[:, :], ALU.add)

    wr = wot.rearrange("(a p) d -> p a d", p=128)
    for i in range(16):
        p.dma([(wtmp[:, :], wr[:, i, :])])
        p.op("vector", "tensor_copy", wbf[:, i, :], wtmp[:, :])
    ln1_r = ln1o.rearrange("(a p) t -> p a t", p=128)
    for tt in range(8):
        p.tr([(tp_ps[:, et, :], comb_all[:, tt, et * 128:(et + 1) * 128], idb_sb[:, :])
              for et in range(16)])
        p.op("vector", "tensor_copy", ctile[:, :, :], tp_ps[:, :, :])
        for ch in range(4):
            p.mm([(sc_ps[:, ch * 512:(ch + 1) * 512],
                   ctile[:, et, :], wbf[:, et, ch * 512:(ch + 1) * 512],
                   et == 0, et == 15) for et in range(16)])
        p.dma([(xt[:, :], x_in[tt * 128:(tt + 1) * 128, :])])
        p.op("vector", "tensor_tensor", x1_sb[:, :], xt[:, :], sc_ps[:, 0:D], ALU.add)
        p.dma([(x1o[tt * 128:(tt + 1) * 128, :], x1_sb[:, :])])
        layernorm(p, ln[:, :], x1_sb[:, :], wtmp[:, :], msum[:, :], m[:, :],
                  vs[:, :], sd[:, :], rstd[:, :], D, eps_sb[:, :])
        p.op("vector", "tensor_copy", ln_bf[:, :], ln[:, :])
        p.tr([(tp_ps[:, j, :], ln_bf[:, j * 128:(j + 1) * 128], idb_sb[:, :])
              for j in range(16)])
        p.op("vector", "tensor_copy", lt_sb[:, :, :], tp_ps[:, :, :])
        p.dma([(ln1_r[:, :, tt * 128:(tt + 1) * 128], lt_sb[:, :, :])])

    p.flush(st)
    per.close()
    return p.done()



def build_D():
    p = Prog()
    nc = p.nc
    st = ExitStack()
    sb = lambda n, s, d: st.enter_context(nc.sbuf_tensor(n, s, d))
    ps = lambda n, s, d: st.enter_context(nc.psum_tensor(n, s, d))

    pg = nc.dram_tensor("pg", [4 * D, TPC], F32, kind="ExternalInput").ap()
    cg = nc.dram_tensor("cg", [4 * D, TPC], BF16, kind="ExternalInput").ap()
    pu = nc.dram_tensor("pu", [4 * D, TPC], F32, kind="ExternalInput").ap()
    pd = nc.dram_tensor("pd", [4 * TPC, D], F32, kind="ExternalInput").ap()
    cd = nc.dram_tensor("cd", [4 * TPC, D], BF16, kind="ExternalInput").ap()
    lnt = nc.dram_tensor("lnt", [D, TOK], BF16, kind="ExternalInput").ap()
    part = nc.dram_tensor("part", [TOK, D], F32, kind="ExternalOutput").ap()

    pbuf = [sb(f"pb{k}", [128, D], F32) for k in range(4)]
    cbuf = [sb(f"cb{k}", [128, D], BF16) for k in range(4)]
    cf = sb("cf", [128, D], F32)
    acc = sb("acc", [128, D], F32)
    tmp = sb("tmp", [128, D], F32)
    wg = sb("wg", [128, 16, TPC], BF16)
    wu = sb("wu", [128, 16, TPC], BF16)
    wd = sb("wd", [128, 8, D], BF16)
    lnb = sb("lnb", [128, 16, 512], BF16)
    sil = sb("sil", [128, 512], F32)
    inter = sb("inter", [128, 8, 512], BF16)
    outsb = sb("outsb", [128, D], F32)

    g_ps = ps("g_ps", [128, 512], F32)
    u_ps = ps("u_ps", [128, 512], F32)
    o_ps = ps("o_ps", [128, D], F32)

    def recon(pdram, cdram, wdst, ht, width):
        p.dma([(pbuf[k][:, 0:width], pdram[k * (pdram.shape[0] // 4) + ht * 128:
                                           k * (pdram.shape[0] // 4) + (ht + 1) * 128, :])
               for k in range(4)])
        p.dma([(cbuf[k][:, 0:width], cdram[k * (cdram.shape[0] // 4) + ht * 128:
                                           k * (cdram.shape[0] // 4) + (ht + 1) * 128, :])
               for k in range(4)])
        p.op("vector", "tensor_copy", cf[:, 0:width], cbuf[0][:, 0:width])
        p.op("vector", "tensor_tensor", acc[:, 0:width], pbuf[0][:, 0:width], cf[:, 0:width], ALU.mult)
        for k in range(1, 4):
            p.op("vector", "tensor_copy", cf[:, 0:width], cbuf[k][:, 0:width])
            p.op("vector", "tensor_tensor", tmp[:, 0:width], pbuf[k][:, 0:width], cf[:, 0:width], ALU.mult)
            p.op("vector", "tensor_tensor", acc[:, 0:width], acc[:, 0:width], tmp[:, 0:width], ALU.add)
        p.op("vector", "tensor_copy", wdst, acc[:, 0:width])

    for ht in range(16):
        recon(pg, cg, wg[:, ht, :], ht, TPC)
    for ht in range(16):
        recon(pu, cg, wu[:, ht, :], ht, TPC)
    for it in range(8):
        recon(pd, cd, wd[:, it, :], it, D)

    lnt_r = lnt.rearrange("(a p) t -> p a t", p=128)
    for ck in range(16):
        tsl = slice(ck * 512, (ck + 1) * 512)
        p.dma([(lnb[:, :, :], lnt_r[:, :, tsl])])
        for ic in range(8):
            ics = slice(ic * 128, (ic + 1) * 128)
            p.mm([(g_ps[:, :], wg[:, ht, ics], lnb[:, ht, :], ht == 0, ht == 15)
                  for ht in range(16)])
            p.mm([(u_ps[:, :], wu[:, ht, ics], lnb[:, ht, :], ht == 0, ht == 15)
                  for ht in range(16)])
            p.op("scalar", "activation", sil[:, :], g_ps[:, :], AF.Silu)
            p.op("vector", "tensor_tensor", inter[:, ic, :], sil[:, :], u_ps[:, :], ALU.mult)
        for tc in range(4):
            for hc in range(4):
                p.mm([(o_ps[:, hc * 512:(hc + 1) * 512],
                       inter[:, it, tc * 128:(tc + 1) * 128],
                       wd[:, it, hc * 512:(hc + 1) * 512],
                       it == 0, it == 7) for it in range(8)])
            p.op("vector", "tensor_copy", outsb[:, :], o_ps[:, :])
            r0 = ck * 512 + tc * 128
            p.dma([(part[r0:r0 + 128, :], outsb[:, :])])
    p.flush(st)
    return p.done()


def build_EF():
    p = Prog()
    nc = p.nc
    st = ExitStack()
    sb = lambda n, s, d: st.enter_context(nc.sbuf_tensor(n, s, d))
    ps = lambda n, s, d: st.enter_context(nc.psum_tensor(n, s, d))
    QKV = 3072

    x1 = nc.dram_tensor("x1", [TPC, D], F32, kind="ExternalInput").ap()
    parts = nc.dram_tensor("parts", [8 * TPC, D], F32, kind="ExternalInput").ap()
    wqkv = nc.dram_tensor("wqkv", [D, QKV], F32, kind="ExternalInput").ap()
    bqkv = nc.dram_tensor("bqkv", [128, 24], F32, kind="ExternalInput").ap()
    idb = nc.dram_tensor("identb", [128, 128], BF16, kind="ExternalInput").ap()
    x2 = nc.dram_tensor("x2", [TPC, D], F32, kind="ExternalOutput").ap()
    qkvt = nc.dram_tensor("qkvt", [QKV, TPC], F32, kind="ExternalOutput").ap()

    acc = sb("acc", [128, D], F32)
    ptmp = sb("ptmp", [128, D], F32)
    ln = sb("ln", [128, D], F32)
    sq = sb("sq", [128, D], F32)
    msum = sb("msum", [128, 1], F32)
    m = sb("m", [128, 1], F32)
    vs = sb("vs", [128, 1], F32)
    sd = sb("sd", [128, 1], F32)
    rstd = sb("rstd", [128, 1], F32)
    eps_sb = sb("eps_sb", [128, 1], F32)
    ln_bf = sb("ln_bf", [128, D], BF16)
    lnT = sb("lnT", [128, 16, TPC], BF16)
    idb_sb = sb("idb_sb", [128, 128], BF16)
    wtmp = sb("wtmp", [128, QKV], F32)
    wbf = sb("wbf", [128, 16, QKV], BF16)
    b_sb = sb("b_sb", [128, 24], F32)
    q_sb = sb("q_sb", [128, TPC], F32)

    qk_ps = ps("qk_ps", [128, TPC], F32)
    tp_ps = ps("tp_ps", [128, 16, 128], BF16)

    p.dma([(idb_sb[:, :], idb), (b_sb[:, :], bqkv)])
    p.op("vector", "memset", eps_sb[:, :], EPS)
    for tt in range(8):
        p.dma([(acc[:, :], x1[tt * 128:(tt + 1) * 128, :])])
        for d_ in range(8):
            p.dma([(ptmp[:, :], parts[d_ * TPC + tt * 128:d_ * TPC + (tt + 1) * 128, :])])
            p.op("vector", "tensor_tensor", acc[:, :], acc[:, :], ptmp[:, :], ALU.add)
        p.dma([(x2[tt * 128:(tt + 1) * 128, :], acc[:, :])])
        layernorm(p, ln[:, :], acc[:, :], sq[:, :], msum[:, :], m[:, :],
                  vs[:, :], sd[:, :], rstd[:, :], D, eps_sb[:, :])
        p.op("vector", "tensor_copy", ln_bf[:, :], ln[:, :])
        p.tr([(tp_ps[:, j, :], ln_bf[:, j * 128:(j + 1) * 128], idb_sb[:, :])
              for j in range(16)])
        p.op("vector", "tensor_copy", lnT[:, :, tt * 128:(tt + 1) * 128], tp_ps[:, :, :])

    wr = wqkv.rearrange("(a p) f -> p a f", p=128)
    for i in range(16):
        p.dma([(wtmp[:, :], wr[:, i, :])])
        p.op("vector", "tensor_copy", wbf[:, i, :], wtmp[:, :])
    for fc in range(24):
        fs = slice(fc * 128, (fc + 1) * 128)
        for half in range(2):
            hs = slice(half * 512, (half + 1) * 512)
            p.mm([(qk_ps[:, hs], wbf[:, dt, fs], lnT[:, dt, hs], dt == 0, dt == 15)
                  for dt in range(16)])
        p.op("vector", "tensor_scalar", q_sb[:, :], qk_ps[:, :], b_sb[:, fc:fc + 1], None, ALU.add)
        p.dma([(qkvt[fc * 128:(fc + 1) * 128, :], q_sb[:, :])])
    p.flush(st)
    return p.done()



def build_GHI():
    p = Prog()
    nc = p.nc
    _uid = [0]

    def _mk(stack, kind, n, s, d):
        _uid[0] += 1
        f = nc.sbuf_tensor if kind == "sb" else nc.psum_tensor
        return stack.enter_context(f(f"{n}_{_uid[0]}", s, d))

    EI = lambda n, s, d: nc.dram_tensor(n, s, d, kind="ExternalInput").ap()
    IN = lambda n, s, d: nc.dram_tensor(n, s, d, kind="Internal").ap()
    qloc = EI("qloc", [D, TPC], F32)
    kvf = EI("kvf", [1024, S], F32)
    wo1 = EI("wo1", [D, D], F32)
    bo1 = EI("bo1", [128, D], F32)
    x2e = EI("x2e", [TPC, D], F32)
    wt1 = EI("wt1", [D, D], F32)
    rt1 = EI("rt1", [D, 8], F32)
    ast1 = EI("ast1", [D, 128], F32)
    bst1 = EI("bst1", [128, D], F32)
    idb = EI("identb", [128, 128], BF16)
    out_e = nc.dram_tensor("out", [TPC, D], F32, kind="ExternalOutput").ap()
    x3l = IN("x3l", [TPC, D], F32)

    per = ExitStack()
    idb_sb = per.enter_context(nc.sbuf_tensor("idb_sb", [128, 128], BF16))
    eps_sb = per.enter_context(nc.sbuf_tensor("eps_sb", [128, 1], F32))
    p.dma([(idb_sb[:, :], idb)])
    p.op("vector", "memset", eps_sb[:, :], EPS)
    st = ExitStack()
    sb = lambda n, s, d: _mk(st, "sb", n, s, d)
    ps = lambda n, s, d: _mk(st, "ps", n, s, d)
    HD = 64
    SC1 = 0.125
    qT_bf = sb("qT_bf", [64, TPC], BF16)
    kT_bf = sb("kT_bf", [64, S], BF16)
    vT_bf = sb("vT_bf", [64, S], BF16)

    v_bf = sb("v_bf", [128, 16, HD], BF16)
    P_sb = sb("P_sb", [128, S], BF16)
    PT_sb = sb("PT_sb", [128, 16, 128], BF16)
    rm = sb("rm", [128, 1], F32)
    rmb = sb("rmb", [128, 1], F32)
    den = sb("den", [128, 1], F32)
    dr = sb("dr", [128, 1], F32)
    comb_all = sb("comb_all", [128, 8, D], BF16)
    wtmp = sb("wtmp", [128, D], F32)
    wbf = sb("wbf", [128, 16, D], BF16)
    ctile = sb("ctile", [128, 16, 128], BF16)
    xt = sb("xt", [128, D], F32)
    x3_sb = sb("x3_sb", [128, D], F32)
    bo_sb = sb("bo_sb", [128, D], F32)
    ln = sb("ln", [128, D], F32)
    qf32 = bo_sb[0:64, 0:TPC]
    kf32 = xt[0:64, :]
    vf32 = x3_sb[0:64, :]
    msum = sb("msum", [128, 1], F32)
    m = sb("m", [128, 1], F32)
    vs = sb("vs", [128, 1], F32)
    sd = sb("sd", [128, 1], F32)
    rstd = sb("rstd", [128, 1], F32)
    ln_bf = sb("ln_bf", [128, D], BF16)
    lnT = sb("lnT", [128, 16, TPC], BF16)
    rt_bf = sb("rt_bf", [128, 16, 8], BF16)
    a_bf = sb("a_bf", [128, 16, 128], BF16)
    b_bf = sb("b_bf", [128, D], BF16)
    sm_tmp = xt[:, :].rearrange("p (a b) -> p a b", a=16)
    probs = sb("probs", [128, 8], F32)
    p1 = sb("p1", [128, 1], F32)
    p2 = sb("p2", [128, 1], F32)
    msk = sb("msk", [128, 8], F32)
    pm = sb("pm", [128, 8], F32)
    wr_sb = sb("wr_sb", [128, 8], F32)
    wd_sb = sb("wd_sb", [128, 8], F32)
    hw_sb = sb("hw_sb", [128, 128], BF16)
    hwT_sb = sb("hwT_sb", [128, 128], BF16)
    sc_ps = ps("sc_ps", [128, S], F32)
    tp_ps = ps("tp_ps", [128, 16, 128], BF16)
    sm_ps = ps("sm_ps", [128, 512], F32)

    for qh in range(32):
        kvh = qh // 4
        p.dma([
            (qf32, qloc[qh * HD:(qh + 1) * HD, :]),
            (kf32, kvf[kvh * HD:(kvh + 1) * HD, :]),
            (vf32, kvf[512 + kvh * HD:512 + (kvh + 1) * HD, :]),
        ])
        p.op("vector", "tensor_copy", qT_bf[:, :], qf32)
        p.op("vector", "tensor_copy", kT_bf[:, :], kf32)
        p.op("vector", "tensor_copy", vT_bf[:, :], vf32)
        p.tr([(tp_ps[:, j, 0:HD], vT_bf[:, j * 128:(j + 1) * 128],
               idb_sb[0:HD, 0:HD]) for j in range(16)])
        p.op("vector", "tensor_copy", v_bf[:, :, :], tp_ps[:, :, 0:HD])
        for c in range(8):
            cs = slice(c * 128, (c + 1) * 128)
            p.mm([(sc_ps[:, j * 512:(j + 1) * 512], qT_bf[:, cs],
                   kT_bf[:, j * 512:(j + 1) * 512], True, True) for j in range(4)])
            p.op("vector", "reduce_max", rm[:, :], sc_ps[:, :], AX.X, negate=True)
            p.op("vector", "tensor_scalar", rmb[:, :], rm[:, :], SC1, None, ALU.mult)
            p.op("scalar", "activation", P_sb[:, :], sc_ps[:, :], AF.Exp,
                 bias=rmb[:, :], scale=SC1)
            p.op("vector", "reduce_sum", den[:, :], P_sb[:, :], AX.X)
            p.tr([(tp_ps[:, j, :], P_sb[:, j * 128:(j + 1) * 128], idb_sb[:, :])
                  for j in range(16)])
            p.op("vector", "tensor_copy", PT_sb[:, :, :], tp_ps[:, :, :])
            p.mm([(sm_ps[:, 0:HD], PT_sb[:, j, :], v_bf[:, j, :], j == 0, j == 15)
                  for j in range(16)])
            p.op("vector", "reciprocal", dr[:, :], den[:, :])
            p.op("vector", "tensor_scalar", comb_all[:, c, qh * HD:(qh + 1) * HD],
                 sm_ps[:, 0:HD], dr[:, :], None, ALU.mult)

    wr = wo1.rearrange("(a p) d -> p a d", p=128)
    for i in range(16):
        p.dma([(wtmp[:, :], wr[:, i, :])])
        p.op("vector", "tensor_copy", wbf[:, i, :], wtmp[:, :])
    p.dma([(bo_sb[:, :], bo1)])
    for tt in range(8):
        ts_ = slice(tt * 128, (tt + 1) * 128)
        p.tr([(tp_ps[:, et, :], comb_all[:, tt, et * 128:(et + 1) * 128], idb_sb[:, :])
              for et in range(16)])
        p.op("vector", "tensor_copy", ctile[:, :, :], tp_ps[:, :, :])
        for ch in range(4):
            p.mm([(sc_ps[:, ch * 512:(ch + 1) * 512],
                   ctile[:, et, :], wbf[:, et, ch * 512:(ch + 1) * 512],
                   et == 0, et == 15) for et in range(16)])
        p.dma([(xt[:, :], x2e[ts_, :])])
        p.op("vector", "tensor_tensor", x3_sb[:, :], xt[:, :], sc_ps[:, 0:D], ALU.add)
        p.op("vector", "tensor_tensor", x3_sb[:, :], x3_sb[:, :], bo_sb[:, :], ALU.add)
        p.dma([(x3l[ts_, :], x3_sb[:, :])])
        layernorm(p, ln[:, :], x3_sb[:, :], wtmp[:, :], msum[:, :], m[:, :],
                  vs[:, :], sd[:, :], rstd[:, :], D, eps_sb[:, :])
        p.op("vector", "tensor_copy", ln_bf[:, :], ln[:, :])
        p.tr([(tp_ps[:, j, :], ln_bf[:, j * 128:(j + 1) * 128], idb_sb[:, :])
              for j in range(16)])
        p.op("vector", "tensor_copy", lnT[:, :, ts_], tp_ps[:, :, :])

    wr2 = wt1.rearrange("(a p) d -> p a d", p=128)
    for i in range(16):
        p.dma([(wtmp[:, :], wr2[:, i, :])])
        p.op("vector", "tensor_copy", wbf[:, i, :], wtmp[:, :])
    rt_r = rt1.rearrange("(a p) e -> p a e", p=128)
    p.dma([(sm_tmp[:, :, 0:8], rt_r)])
    p.op("vector", "tensor_copy", rt_bf[:, :, :], sm_tmp[:, :, 0:8])
    ast_r = ast1.rearrange("(a p) r -> p a r", p=128)
    p.dma([(sm_tmp, ast_r)])
    p.op("vector", "tensor_copy", a_bf[:, :, :], sm_tmp)
    p.dma([(wtmp[:, :], bst1)])
    p.op("vector", "tensor_copy", b_bf[:, :], wtmp[:, :])

    for tt in range(8):
        ts_ = slice(tt * 128, (tt + 1) * 128)
        for ch in range(4):
            p.mm([(sc_ps[:, ch * 512:(ch + 1) * 512],
                   lnT[:, dt, ts_], wbf[:, dt, ch * 512:(ch + 1) * 512],
                   dt == 0, False) for dt in range(16)])
        p.mm([(sm_ps[:, 64:72], lnT[:, dt, ts_], rt_bf[:, dt, :], dt == 0, dt == 15)
              for dt in range(16)])
        p.mm([(sm_ps[:, 128:256], lnT[:, dt, ts_], a_bf[:, dt, :], dt == 0, dt == 15)
              for dt in range(16)])
        p.op("vector", "reduce_max", p1[:, :], sm_ps[:, 64:72], AX.X, negate=True)
        p.op("scalar", "activation", probs[:, :], sm_ps[:, 64:72], AF.Exp,
             bias=p1[:, :], scale=1.0)
        p.op("vector", "reduce_sum", p2[:, :], probs[:, :], AX.X)
        p.op("vector", "reciprocal", p2[:, :], p2[:, :])
        p.op("vector", "tensor_scalar", probs[:, :], probs[:, :], p2[:, :], None, ALU.mult)
        p.op("vector", "reduce_max", p1[:, :], probs[:, :], AX.X)
        p.op("vector", "tensor_scalar", msk[:, :], probs[:, :], p1[:, :], None, ALU.is_ge)
        p.op("vector", "tensor_tensor", pm[:, :], probs[:, :], msk[:, :], ALU.mult)
        p.op("vector", "tensor_tensor", pm[:, :], probs[:, :], pm[:, :], ALU.subtract)
        p.op("vector", "reduce_max", p2[:, :], pm[:, :], AX.X)
        p.op("vector", "tensor_scalar", msk[:, :], probs[:, :], p2[:, :], None, ALU.is_ge)
        p.op("vector", "tensor_tensor", wr_sb[:, :], probs[:, :], msk[:, :], ALU.mult)
        p.op("vector", "reduce_sum", p2[:, :], wr_sb[:, :], AX.X)
        p.op("vector", "reciprocal", p2[:, :], p2[:, :])
        p.op("vector", "tensor_scalar", wd_sb[:, :], wr_sb[:, :], p2[:, :], None, ALU.mult)
        p.op("vector", "tensor_scalar", wd_sb[:, :], wd_sb[:, :], 2.0, None, ALU.mult)
        for e in range(8):
            p.op("vector", "tensor_scalar", hw_sb[:, e * 16:(e + 1) * 16],
                 sm_ps[:, 128 + e * 16:128 + (e + 1) * 16], wd_sb[:, e:e + 1], None, ALU.mult)
        p.tr([(tp_ps[:, 0, :], hw_sb[:, :], idb_sb[:, :])])
        p.op("vector", "tensor_copy", hwT_sb[:, :], tp_ps[:, 0, :])
        p.mm([(sc_ps[:, ch * 512:(ch + 1) * 512], hwT_sb[:, :],
               b_bf[:, ch * 512:(ch + 1) * 512], False, True) for ch in range(4)])
        p.dma([(xt[:, :], x3l[ts_, :])])
        p.op("vector", "tensor_tensor", ln[:, :], xt[:, :], sc_ps[:, 0:D], ALU.add)
        p.dma([(out_e[ts_, :], ln[:, :])])
    p.flush(st)
    per.close()
    return p.done()



def _catmull_grid(n_ctrl, n):
    t = np.linspace(0.0, n_ctrl - 1.0, n, dtype=np.float64)
    i = np.clip(np.floor(t).astype(np.int64), 0, n_ctrl - 2)
    f = (t - i).astype(np.float32)
    idx = np.stack([np.clip(i + k, 0, n_ctrl - 1) for k in (-1, 0, 1, 2)], axis=0)
    return idx.astype(np.int64), f


def kernel(L0_attn_hidden_states, L0_attn_Wq, L0_attn_Wk, L0_attn_Wv, L0_attn_Wo,
           L0_attn_gate, L0_ffn_gate_cp, L0_ffn_up_cp, L0_ffn_down_cp,
           L1_attn_Wqkv, L1_attn_bqkv, L1_attn_Wo, L1_attn_bo,
           L1_ffn_weight, L1_ffn_lora_A, L1_ffn_lora_B, L1_ffn_router_w):
    cid = list(range(NC))
    af32 = lambda a: np.ascontiguousarray(np.asarray(a, dtype=np.float32))

    x0 = af32(L0_attn_hidden_states).reshape(TOK, D)
    identb = np.eye(128, dtype=BF)

    inv_freq = (1.0 / 10000.0 ** (np.arange(0, D, 2, dtype=np.float32) / D)).astype(np.float32)
    tpos = np.arange(S, dtype=np.float32)
    emb = np.concatenate([tpos[:, None] * inv_freq[None, :]] * 2, axis=-1)
    cosf = np.cos(emb).astype(np.float32)
    sinf = np.sin(emb).astype(np.float32)
    cosT = np.ascontiguousarray(cosf.T)
    sinT = np.ascontiguousarray(sinf.T)

    wqt = af32(np.asarray(L0_attn_Wq).T)
    wkt = af32(np.asarray(L0_attn_Wk).T)
    wvt = af32(np.asarray(L0_attn_Wv).T)
    wot = af32(np.asarray(L0_attn_Wo).T)
    gv = af32(L0_attn_gate).reshape(16)
    gate_in = np.repeat(gv, 128).reshape(16 * 128, 1).astype(np.float32)

    nc = build_ABC()
    maps = []
    for c in range(NC):
        b, hc = c // 2, c % 2
        own = slice(b * S + hc * TPC, b * S + (hc + 1) * TPC)
        par = slice(b * S + (1 - hc) * TPC, b * S + (2 - hc) * TPC)
        xcat = np.concatenate([x0[own], x0[par]], 0)
        s_own = slice(hc * TPC, (hc + 1) * TPC)
        s_par = slice((1 - hc) * TPC, (2 - hc) * TPC)
        cosp = np.concatenate([cosf[s_own], cosf[s_par]], 0)
        sinp = np.concatenate([sinf[s_own], sinf[s_par]], 0)
        cosTp = np.concatenate([cosT[:, s_own], cosT[:, s_par]], 1)
        sinTp = np.concatenate([sinT[:, s_own], sinT[:, s_par]], 1)
        maps.append({
            "x": np.ascontiguousarray(xcat),
            "cosn": np.ascontiguousarray(cosp),
            "sinn": np.ascontiguousarray(sinp),
            "cost": np.ascontiguousarray(cosTp[:, 0:TPC]),
            "sint": np.ascontiguousarray(sinTp[:, 0:TPC]),
            "costb": np.ascontiguousarray(cosTp),
            "sintb": np.ascontiguousarray(sinTp),
            "wqt": wqt, "wkt": wkt, "wvt": wvt, "wot": wot,
            "gate": gate_in, "identb": identb,
        })
    res = run_bass_kernel_spmd(nc, maps, core_ids=cid)
    X1 = np.concatenate([np.asarray(r["x1o"]) for r in res.results], axis=0)
    LN1T = np.ascontiguousarray(np.concatenate(
        [np.asarray(r["ln1o"]) for r in res.results], axis=1))

    INTER = 8192
    n_ctrl = max(16, int(INTER * D / 128.9))
    idxg, fg = _catmull_grid(n_ctrl, INTER * D)
    f2 = fg * fg
    f3 = f2 * fg
    cks = [0.5 * (-fg + 2 * f2 - f3), 0.5 * (2 - 5 * f2 + 3 * f3),
           0.5 * (fg + 4 * f2 - 3 * f3), 0.5 * (-f2 + f3)]
    cks = [c.astype(np.float32) for c in cks]
    gcp = af32(L0_ffn_gate_cp)
    ucp = af32(L0_ffn_up_cp)
    dcp = af32(L0_ffn_down_cp)
    pg_c = [[] for _ in range(NC)]
    pu_c = [[] for _ in range(NC)]
    cg_c = [[] for _ in range(NC)]
    for k in range(4):
        pk_g = gcp[idxg[k]]
        pk_u = ucp[idxg[k]]
        ck = cks[k]
        for c in range(NC):
            lo, hi = c * 1024 * D, (c + 1) * 1024 * D
            pg_c[c].append(np.ascontiguousarray(pk_g[lo:hi].reshape(1024, D).T))
            pu_c[c].append(np.ascontiguousarray(pk_u[lo:hi].reshape(1024, D).T))
            cg_c[c].append(np.ascontiguousarray(ck[lo:hi].reshape(1024, D).T.astype(BF)))
    pd_c = [[] for _ in range(NC)]
    cd_c = [[] for _ in range(NC)]
    for k in range(4):
        pk_d = dcp[idxg[k]].reshape(D, INTER)
        ck = cks[k].reshape(D, INTER)
        for c in range(NC):
            sl = slice(c * 1024, (c + 1) * 1024)
            pd_c[c].append(np.ascontiguousarray(pk_d[:, sl].T))
            cd_c[c].append(np.ascontiguousarray(ck[:, sl].T.astype(BF)))
    nc = build_D()
    maps = [{"pg": np.concatenate(pg_c[c], 0), "cg": np.concatenate(cg_c[c], 0),
             "pu": np.concatenate(pu_c[c], 0),
             "pd": np.concatenate(pd_c[c], 0), "cd": np.concatenate(cd_c[c], 0),
             "lnt": LN1T} for c in range(NC)]
    res = run_bass_kernel_spmd(nc, maps, core_ids=cid)
    parts = [np.asarray(r["part"]) for r in res.results]

    bq = np.ascontiguousarray(af32(L1_attn_bqkv).reshape(24, 128).T)
    wqkv = af32(L1_attn_Wqkv)
    nc = build_EF()
    maps = []
    for c in range(NC):
        pstack = np.concatenate([parts[d_][c * TPC:(c + 1) * TPC] for d_ in range(NC)], 0)
        maps.append({"x1": X1[c * TPC:(c + 1) * TPC], "parts": pstack,
                     "wqkv": wqkv, "bqkv": bq, "identb": identb})
    res = run_bass_kernel_spmd(nc, maps, core_ids=cid)
    X2 = np.concatenate([np.asarray(r["x2"]) for r in res.results], axis=0)
    QKVT = np.ascontiguousarray(np.concatenate(
        [np.asarray(r["qkvt"]) for r in res.results], axis=1))

    wo1 = af32(L1_attn_Wo)
    bo_bc = np.ascontiguousarray(np.broadcast_to(af32(L1_attn_bo), (128, D)))
    wt_rhs = af32(np.asarray(L1_ffn_weight).T)
    rt_h = af32(np.asarray(L1_ffn_router_w).T)
    a_st = af32(np.asarray(L1_ffn_lora_A).transpose(2, 0, 1).reshape(D, 128))
    b_st = af32(np.asarray(L1_ffn_lora_B).transpose(0, 2, 1).reshape(128, D))
    nc = build_GHI()
    maps = []
    for c in range(NC):
        b = c // 2
        maps.append({
            "qloc": np.ascontiguousarray(QKVT[0:D, c * TPC:(c + 1) * TPC]),
            "kvf": np.ascontiguousarray(QKVT[D:3072, b * S:(b + 1) * S]),
            "wo1": wo1, "bo1": bo_bc, "x2e": X2[c * TPC:(c + 1) * TPC],
            "wt1": wt_rhs, "rt1": rt_h, "ast1": a_st, "bst1": b_st,
            "identb": identb,
        })
    res = run_bass_kernel_spmd(nc, maps, core_ids=cid)
    out = np.concatenate([np.asarray(r["out"]) for r in res.results], axis=0)
    return np.ascontiguousarray(out.reshape(4, S, D).astype(np.float32))
